# version 29
# baseline (speedup 1.0000x reference)
import contextlib

import numpy as np

import concourse.bass as bass
import concourse.mybir as mybir
from concourse.bass_utils import run_bass_kernel_spmd

F32 = mybir.dt.float32
BF16 = mybir.dt.bfloat16
AF = mybir.ActivationFunctionType
ALU = mybir.AluOpType
AX = mybir.AxisListType.X
f32 = np.float32

P = 128
C = 1024
CI = 8
NSEG = 2
NR = 4
NQ = 4
NT = NR * NQ
E = 8
ISH = 256
EXPB = f32(-8.0)
EPS = f32(1e-5)
PT_OFF = [0, 512, 1024, 1280]

ENGS = ("sync", "gpsimd", "tensor", "vector", "scalar")


def build_nc():
    nc = bass.Bass("TRN2", target_bir_lowering=False)

    dx = nc.dram_tensor("x", [NSEG * 2048, C], F32, kind="ExternalInput").ap()
    dout = nc.dram_tensor("out", [NSEG * 2048, C], F32, kind="ExternalOutput").ap()
    dwin = nc.dram_tensor("w_in", [3 * C, C], F32, kind="ExternalInput").ap()
    dbin = nc.dram_tensor("b_in", [1, 3 * C], F32, kind="ExternalInput").ap()
    dwout = nc.dram_tensor("w_out", [C, C], F32, kind="ExternalInput").ap()
    dbout = nc.dram_tensor("b_out", [1, C], F32, kind="ExternalInput").ap()
    dw1 = nc.dram_tensor("w1", [E, C, 128], F32, kind="ExternalInput").ap()
    dw2 = nc.dram_tensor("w2", [E, C, 128], F32, kind="ExternalInput").ap()
    dw3 = nc.dram_tensor("w3", [E, 128, C], F32, kind="ExternalInput").ap()
    dws1 = nc.dram_tensor("ws1", [C, ISH], F32, kind="ExternalInput").ap()
    dws2 = nc.dram_tensor("ws2", [C, ISH], F32, kind="ExternalInput").ap()
    dws3 = nc.dram_tensor("ws3", [ISH, C], F32, kind="ExternalInput").ap()
    dwr = nc.dram_tensor("wr", [C, E], F32, kind="ExternalInput").ap()

    xv = dx.rearrange("(s q p r) c -> s r q p c", s=NSEG, q=NQ, p=P, r=NR)
    ov = dout.rearrange("(s q p r) c -> s r q p c", s=NSEG, q=NQ, p=P, r=NR)

    ctx = contextlib.ExitStack()
    with ctx:
        A = ctx.enter_context

        Wqk = A(nc.sbuf_tensor("Wqk", [P, CI, 2048], BF16))
        Wv = A(nc.sbuf_tensor("Wv", [P, CI, 1024], BF16))
        Wo = A(nc.sbuf_tensor("Wo", [P, CI, 1024], BF16))
        W1 = A(nc.sbuf_tensor("W1", [P, E, CI, 128], BF16))
        W2 = A(nc.sbuf_tensor("W2", [P, E, CI, 128], BF16))
        W3 = A(nc.sbuf_tensor("W3", [P, E, 1024], BF16))
        Ws1 = A(nc.sbuf_tensor("Ws1", [P, CI, ISH], BF16))
        Ws2 = A(nc.sbuf_tensor("Ws2", [P, CI, ISH], BF16))
        Ws3 = A(nc.sbuf_tensor("Ws3", [P, 2, 1024], BF16))
        Wr = A(nc.sbuf_tensor("Wr", [P, CI, E], BF16))
        WinNat = A(nc.sbuf_tensor("WinNat", [P, 2, 1024], BF16))
        brow = A(nc.sbuf_tensor("brow", [1, 4 * C], BF16))

        ident = A(nc.sbuf_tensor("ident", [P, P], BF16))
        utri = A(nc.sbuf_tensor("utri", [P, P], BF16))
        ones1 = A(nc.sbuf_tensor("ones1", [1, P], BF16))
        onestok = A(nc.sbuf_tensor("onestok", [1, 512], BF16))
        onescol = A(nc.sbuf_tensor("onescol", [P, 1], BF16))

        XF = A(nc.sbuf_tensor("XF", [P, 2, C], F32))
        xn_tm = A(nc.sbuf_tensor("xn_tm", [P, 2, C], BF16))
        XN = A(nc.sbuf_tensor("XN", [P, 2, CI, 512], BF16))
        st_s = A(nc.sbuf_tensor("st_s", [P, NSEG, NT], F32))
        st_ss = A(nc.sbuf_tensor("st_ss", [P, NSEG, NT], F32))
        st_m = A(nc.sbuf_tensor("st_m", [P, NSEG, NT], F32))
        st_m2 = A(nc.sbuf_tensor("st_m2", [P, NSEG, NT], F32))
        st_v = A(nc.sbuf_tensor("st_v", [P, NSEG, NT], F32))
        st_r = A(nc.sbuf_tensor("st_r", [P, NSEG, NT], F32))

        Q2 = A(nc.sbuf_tensor("Q2", [P, 512], BF16))
        K2 = A(nc.sbuf_tensor("K2", [P, 512], BF16))
        Vsb = A(nc.sbuf_tensor("Vsb", [P, NQ, 512], BF16))
        pT = A(nc.sbuf_tensor("pT", [P, 2, 1536], BF16))
        OT = A(nc.sbuf_tensor("OT", [P, NQ, P], BF16))
        OFM = A(nc.sbuf_tensor("OFM", [P, CI, 512], BF16))
        rz = A(nc.sbuf_tensor("rz", [P, 2, NQ], F32))

        zrt = A(nc.sbuf_tensor("zrt", [P, NQ, E], F32))
        Grt = A(nc.sbuf_tensor("Grt", [P, NQ, E, E], BF16))
        rankb = A(nc.sbuf_tensor("rankb", [P, NQ, E], F32))
        keepb = A(nc.sbuf_tensor("keepb", [P, NQ, E], F32))
        wgt = A(nc.sbuf_tensor("wgt", [P, NQ, E], F32))
        ssb = A(nc.sbuf_tensor("ssb", [P, NQ], F32))
        rsb = A(nc.sbuf_tensor("rsb", [P, NQ], F32))
        gtm = A(nc.sbuf_tensor("gtm", [P, NQ, E], BF16))
        gT = A(nc.sbuf_tensor("gT", [E, NQ, P], BF16))
        HS = A(nc.sbuf_tensor("HS", [P, 2, 512], BF16))
        s1e = A(nc.sbuf_tensor("s1e", [P, 2, 512], BF16))
        HP = A(nc.sbuf_tensor("HP", [P, E, 512], BF16))
        out_sb = A(nc.sbuf_tensor("osb", [P, 2, 512], F32))
        sel = A(nc.sbuf_tensor("sel", [E, E * P], BF16))

        ptT0 = A(nc.psum_tensor("ptT0", [P, CI, P], BF16))
        ptT1 = A(nc.psum_tensor("ptT1", [P, CI, P], BF16))
        pQ = A(nc.psum_tensor("pQ", [P, 512], F32))
        pK = A(nc.psum_tensor("pK", [P, 512], F32))
        pA = A(nc.psum_tensor("pA", [P, 2, 512], F32))
        pB = A(nc.psum_tensor("pB", [P, 2, 512], F32))
        oEv = pQ[:, 0:272].rearrange("p (j k) -> p j k", j=NQ, k=68)
        oOd = pK[:, 0:272].rearrange("p (j k) -> p j k", j=NQ, k=68)
        prt = pB[:, 0, 0:32].rearrange("p (j e) -> p j e", j=NQ, e=E)

        sbuf_left = nc.sbuf_bytes_remaining if isinstance(nc.sbuf_bytes_remaining, int) else nc.sbuf_bytes_remaining()

        sem_names = ["sg", "sp", "sv", "sa", "sb", "swfa", "swfb", "swfc", "swin0", "swin1",
                     "sx0", "sx1", "sacc0", "sacc1", "sacc2", "sacc3",
                     "sst0", "sst1", "sst2", "sst3"]
        S = {n: A(nc.semaphore(n)) for n in sem_names}
        cnt = {n: 0 for n in sem_names}
        sch = {e: [] for e in ENGS}

        def em(eng, fn, waits=(), inc=None, k=1):
            wd = {}
            for item in waits:
                if item is None:
                    continue
                s, v = item
                if v and v > wd.get(s, 0):
                    wd[s] = v
            for s, v in wd.items():
                assert v <= cnt[s], f"future wait {s} {v} > {cnt[s]}"
            wt = tuple(wd.items())

            def run(eng_obj, wt=wt, fn=fn, inc=inc, k=k):
                for s, v in wt:
                    eng_obj.wait_ge(S[s], v)
                ins = fn(eng_obj)
                if inc is not None:
                    ins.then_inc(S[inc], k)

            sch[eng].append(run)
            if inc is not None:
                cnt[inc] += k
                return cnt[inc]
            return None

        bank = {n: None for n in ["T0", "T1", "Q", "K", "A0", "A1", "B0", "B1"]}
        bview = {"T0": ptT0[:, :, :], "T1": ptT1[:, :, :], "Q": pQ[:, :], "K": pK[:, :],
                 "A0": pA[:, 0, :], "A1": pA[:, 1, :],
                 "B0": pB[:, 0, :], "B1": pB[:, 1, :]}

        def bw(*names):
            return [bank[n] for n in names if bank[n] is not None]

        c1 = em("vector", lambda v: v.memset(ident[:], 1.0), inc="sv")
        c2 = em("vector", lambda v: v.memset(utri[:], 1.0), inc="sv")
        em("gpsimd", lambda g: g.affine_select(
            ident[:], ident[:], pattern=[[1, P]], compare_op=ALU.is_equal,
            fill=0.0, base=0, channel_multiplier=-1), waits=[("sv", c1)], inc="sg")
        em("gpsimd", lambda g: g.affine_select(
            utri[:], utri[:], pattern=[[1, P]], compare_op=ALU.is_ge,
            fill=0.0, base=0, channel_multiplier=-1), waits=[("sv", c2)], inc="sg")
        em("vector", lambda v: v.memset(ones1[:], 1.0), inc="sg")
        em("vector", lambda v: v.memset(onestok[:], 1.0), inc="sg")
        c3 = em("vector", lambda v: v.memset(onescol[:], 1.0), inc="sg")
        c4 = em("vector", lambda v: v.memset(sel[:], 1.0), inc="sg")
        c5 = em("gpsimd", lambda g: g.affine_select(
            sel[:], sel[:], pattern=[[1, E * P]], compare_op=ALU.is_ge,
            fill=0.0, base=0, channel_multiplier=-P), waits=[("sg", c4)], inc="sg")
        sg_done = em("gpsimd", lambda g: g.affine_select(
            sel[:], sel[:], pattern=[[-1, E * P]], compare_op=ALU.is_ge,
            fill=0.0, base=P - 1, channel_multiplier=P), waits=[("sg", c5)], inc="sg")

        em("vector", lambda v: v.memset(pQ[:], 0.0), inc="sv")
        em("vector", lambda v: v.memset(pK[:], 0.0), inc="sv")
        em("vector", lambda v: v.memset(pA[:], 0.0), inc="sv")
        ps0 = em("vector", lambda v: v.memset(pB[:], 0.0), inc="sv")

        em("gpsimd", lambda g: g.dma_start(out=brow[:, 0:3 * C], in_=dbin), inc="sb", k=16)
        sb_done = em("gpsimd", lambda g: g.dma_start(out=brow[:, 3 * C:], in_=dbout),
                     inc="sb", k=16)

        wch_ready = {}
        tr_read = {0: None, 1: None}
        worder = [0, 8, 16, 17, 18, 19, 1, 9, 2, 10, 3, 11, 20, 21, 22, 23,
                  4, 12, 5, 13, 6, 14, 7, 15] + list(range(24, 32))
        widx_chunk = {}
        witems = []

        def _mk_chunk_item(m):
            def item():
                mi = wmi[0]
                wmi[0] += 1
                sl = mi % 2
                src_ = (dwin[128 * m:128 * (m + 1), :] if m < 24
                        else dwout[128 * (m - 24):128 * (m - 23), :])
                em("gpsimd",
                   lambda g, sl=sl, src_=src_: g.dma_start(out=WinNat[:, sl, :], in_=src_),
                   waits=[tr_read[sl]], inc="swin%d" % sl, k=16)
                tb = "T%d" % sl

                def mk_wtr(sl=sl, tb=tb):
                    def f(t):
                        ins = None
                        for ci in range(CI):
                            ins = t.transpose(bview[tb][:, ci, :],
                                              WinNat[:, sl, 128 * ci:128 * (ci + 1)],
                                              ident[:])
                        return ins
                    return f
                tr = em("tensor", mk_wtr(),
                        waits=[("swin%d" % sl, 16 * (mi // 2 + 1)), ("sg", sg_done)] + bw(tb),
                        inc="sp")
                tr_read[sl] = ("sp", tr)
                if m < 16:
                    dst = Wqk[:, :, 128 * m:128 * (m + 1)]
                elif m < 24:
                    dst = Wv[:, :, 128 * (m - 16):128 * (m - 15)]
                else:
                    dst = Wo[:, :, 128 * (m - 24):128 * (m - 23)]
                ev = em("vector",
                        lambda v, tb=tb, dst=dst: v.tensor_copy(dst, bview[tb][:, :, :]),
                        waits=[("sp", tr)], inc="sv")
                bank[tb] = ("sv", ev)
                wch_ready[m] = ev
            return item

        def _mk_ffn_item(fn, sem):
            return lambda: em("gpsimd", fn, inc=sem, k=16)

        wmi = [0]
        _ffn_shared = [
            _mk_ffn_item(lambda g: g.dma_start(
                out=Wr[:], in_=dwr.rearrange("(ci p) e -> p ci e", p=P)), "swfa"),
            _mk_ffn_item(lambda g: g.dma_start(
                out=Ws1[:], in_=dws1.rearrange("(ci p) i -> p ci i", p=P)), "swfa"),
            _mk_ffn_item(lambda g: g.dma_start(
                out=Ws2[:], in_=dws2.rearrange("(ci p) i -> p ci i", p=P)), "swfa"),
            _mk_ffn_item(lambda g: g.dma_start(
                out=Ws3[:], in_=dws3.rearrange("(k p) c -> p k c", p=P)), "swfa"),
        ]
        _ffn_exp = []
        for e in range(E):
            _ffn_exp.append(_mk_ffn_item(lambda g, e=e: g.dma_start(
                out=W1[:, e], in_=dw1[e].rearrange("(ci p) i -> p ci i", p=P)), "swfb"))
            _ffn_exp.append(_mk_ffn_item(lambda g, e=e: g.dma_start(
                out=W2[:, e], in_=dw2[e].rearrange("(ci p) i -> p ci i", p=P)), "swfb"))
        _ffn_w3 = [_mk_ffn_item(lambda g: g.dma_start(
            out=W3[:], in_=dw3.rearrange("e p c -> p e c")), "swfc")]

        _chunks = [_mk_chunk_item(m) for m in worder[:24]]
        _wo = [_mk_chunk_item(m) for m in worder[24:]]
        witems = (_chunks[0:6] + _ffn_shared + _chunks[6:8] + _ffn_exp[0:2]
                  + _chunks[8:10] + _ffn_exp[2:4] + _chunks[10:12] + _ffn_exp[4:6]
                  + _chunks[12:16] + _ffn_exp[6:8] + _chunks[16:18] + _ffn_exp[8:10]
                  + _chunks[18:20] + _ffn_exp[10:12] + _chunks[20:22] + _ffn_exp[12:14]
                  + _chunks[22:24] + _ffn_exp[14:16] + _ffn_w3 + _wo)
        for i, m in enumerate(worder):
            pass
        widx = {}
        _pos = {id(c): None for c in _chunks}
        for pos, it in enumerate(witems):
            for ci_, m in zip(_chunks + _wo, worder):
                if it is ci_:
                    widx[m] = pos
        swfa_done = 16 * 4
        swfb_done = 16 * 16
        swfc_done = 16 * 1
        wptr = [0]
        wptr = [0]

        def wensure(n):
            while wptr[0] < n:
                witems[wptr[0]]()
                wptr[0] += 1

        def wpump(n=2):
            wensure(min(wptr[0] + n, len(witems)))

        state = {
            "xf_free": {0: [None], 1: [None]},
            "xf_uses": {0: 0, 1: 0},
            "xn_ready": {},
            "xnbuf_free": {0: None, 1: None},
            "xntm_free": {0: None, 1: None},
            "ofm_ready": {},
            "ofm_free": None, "hp_free": None,
            "hs_free2": {0: None, 1: None}, "vsb_attn_done": None,
            "gt_free": None, "zrt_free": None, "gtm_free": None,
            "ot_free": None, "vsb_free": None, "q2_free": None, "k2_free": None,
            "pt_free": {0: None, 1: None},
            "acc_uses": {0: 0, 1: 0, 2: 0, 3: 0},
            "st_uses": {0: 0, 1: 0, 2: 0, 3: 0}, "fin_slot": 0,
            "tile": 0, "trtile": 0, "vsb_ready": None,
        }

        lninfo = {}

        def ln_pre(seg, j):
            r, q = j // NQ, j % NQ
            gslot = state["tile"] % 2
            xsl = state["tile"] % 2
            state["tile"] += 1
            sxn = "sx%d" % gslot
            state["xf_uses"][gslot] += 1
            uses = state["xf_uses"][gslot]
            em("sync", lambda sy, seg=seg, r=r, q=q, gslot=gslot:
               sy.dma_start(out=XF[:, gslot, :], in_=xv[seg, r, q]),
               waits=state["xf_free"][gslot], inc=sxn, k=16)
            ldw = [(sxn, 16 * uses)]
            crs = em("vector", lambda v, seg=seg, j=j, gslot=gslot: v.reduce_sum(
                st_s[:, seg, j:j + 1], XF[:, gslot, :], axis=AX), waits=ldw, inc="sv")
            csq = em("scalar", lambda s, seg=seg, j=j, gslot=gslot, xsl=xsl: s.activation(
                out=xn_tm[:, xsl, :], in_=XF[:, gslot, :], func=AF.Square,
                accum_out=st_ss[:, seg, j:j + 1]),
                waits=ldw + [state["xntm_free"][xsl]], inc="sa")
            cm = em("scalar", lambda s, seg=seg, j=j: s.activation(
                out=st_m[:, seg, j:j + 1], in_=st_s[:, seg, j:j + 1], func=AF.Identity,
                scale=f32(1.0 / C)), waits=[("sv", crs)], inc="sa")
            cm2 = em("scalar", lambda s, seg=seg, j=j: s.activation(
                out=st_m2[:, seg, j:j + 1], in_=st_m[:, seg, j:j + 1], func=AF.Square),
                waits=[("sa", cm)], inc="sa")
            cv = em("vector", lambda v, seg=seg, j=j: v.scalar_tensor_tensor(
                out=st_v[:, seg, j:j + 1], in0=st_ss[:, seg, j:j + 1], scalar=1.0 / C,
                in1=st_m2[:, seg, j:j + 1], op0=ALU.mult, op1=ALU.subtract),
                waits=[("sa", cm2)], inc="sv")
            cl = em("scalar", lambda s, seg=seg, j=j: s.activation(
                out=st_v[:, seg, j:j + 1], in_=st_v[:, seg, j:j + 1], func=AF.Ln,
                bias=EPS), waits=[("sv", cv)], inc="sa")
            cr = em("scalar", lambda s, seg=seg, j=j: s.activation(
                out=st_r[:, seg, j:j + 1], in_=st_v[:, seg, j:j + 1], func=AF.Exp,
                scale=f32(-0.5)), waits=[("sa", cl)], inc="sa")
            cxn = em("vector", lambda v, seg=seg, j=j, gslot=gslot, xsl=xsl: v.tensor_scalar(
                out=xn_tm[:, xsl, :], in0=XF[:, gslot, :],
                scalar1=st_m[:, seg, j:j + 1], scalar2=st_r[:, seg, j:j + 1],
                op0=ALU.subtract, op1=ALU.mult), waits=[("sa", cr)], inc="sv")
            state["xf_free"][gslot] = [("sv", cxn), ("sa", csq)]
            lninfo[(seg, j)] = {"cxn": cxn, "xsl": xsl}

        def ln_tr(seg, j):
            r, q = j // NQ, j % NQ
            info = lninfo[(seg, j)]
            xsl = info["xsl"]
            state["trtile"] += 1
            tb = "T%d" % (state["trtile"] % 2)

            def mk_tr(xsl=xsl, tb=tb):
                def f(t):
                    ins = None
                    for ci in range(CI):
                        ins = t.transpose(bview[tb][:, ci, :],
                                          xn_tm[:, xsl, 128 * ci:128 * (ci + 1)], ident[:])
                    return ins
                return f
            ctr = em("tensor", mk_tr(), waits=[("sv", info["cxn"])] + bw(tb),
                     inc="sp")
            state["xntm_free"][xsl] = ("sp", ctr)
            rb = r % 2
            evw = [("sp", ctr)] + ([state["xnbuf_free"][rb]] if q == 0 else [])
            cev = em("vector", lambda v, rb=rb, q=q, tb=tb: v.tensor_copy(
                XN[:, rb, :, 128 * q:128 * (q + 1)], bview[tb][:, :, :]),
                waits=evw, inc="sv")
            bank[tb] = ("sv", cev)
            if q == NQ - 1:
                state["xn_ready"][(seg, r)] = cev

        lnq = [(s_, j_) for s_ in range(NSEG) for j_ in range(NT)]
        lnp = {"pre": 0, "tr": 0}
        ffn_done = set()
        prev_user = {}
        for bufid in (0, 1):
            groups = [(s_, r_) for s_ in range(NSEG) for r_ in range(NR) if r_ % 2 == bufid]
            for i_, g_ in enumerate(groups):
                prev_user[g_] = groups[i_ - 1] if i_ > 0 else None

        def can_tr(i):
            if i >= len(lnq) or lnp["pre"] <= i + 1:
                return False
            seg, j = lnq[i]
            pu = prev_user[(seg, j // NQ)]
            return pu is None or pu in ffn_done

        def can_pre(i):
            return i < len(lnq) and (i < 2 or lnp["tr"] >= i - 1)

        def ln_pump(maxtr=1, maxpre=1):
            ntr = npre = 0
            moved = True
            while moved:
                moved = False
                if ntr < maxtr and can_tr(lnp["tr"]):
                    ln_tr(*lnq[lnp["tr"]])
                    lnp["tr"] += 1
                    ntr += 1
                    moved = True
                if npre < maxpre and can_pre(lnp["pre"]):
                    ln_pre(*lnq[lnp["pre"]])
                    lnp["pre"] += 1
                    npre += 1
                    moved = True

        def ln_pump_tr():
            if can_tr(lnp["tr"]):
                ln_tr(*lnq[lnp["tr"]])
                lnp["tr"] += 1

        def ensure_group(seg, r):
            need = (seg * NT + r * NQ) + NQ
            while lnp["tr"] < need:
                before = (lnp["tr"], lnp["pre"])
                if can_pre(lnp["pre"]):
                    ln_pre(*lnq[lnp["pre"]])
                    lnp["pre"] += 1
                if lnp["tr"] < len(lnq) and lnp["pre"] > lnp["tr"]:
                    seg_, j_ = lnq[lnp["tr"]]
                    pu = prev_user[(seg_, j_ // NQ)]
                    if pu is None or pu in ffn_done:
                        ln_tr(seg_, j_)
                        lnp["tr"] += 1
                assert (lnp["tr"], lnp["pre"]) != before, \
                    f"LN pump stuck at {before} needing {need}"

        def qkv_pair(seg, hp):
            wensure(max(widx[hp], widx[8 + hp]) + 1)
            def mkq(hp=hp):
                def f(t):
                    for ci in range(CI):
                        t.matmul(pA[:, 0, :], Wqk[:, ci, 128 * hp:128 * (hp + 1)],
                                 XN[:, 0, ci, :], start=(ci == 0), stop=False)
                    return t.matmul(pA[:, 0, :], brow[0:1, 128 * hp:128 * (hp + 1)],
                                    onestok[0:1, :], start=False, stop=True)
                return f
            cq = em("tensor", mkq(),
                    waits=[("sv", state["xn_ready"][(seg, 0)]), ("sv", wch_ready[hp]),
                           ("sb", sb_done), ("sg", sg_done)] + bw("A0"), inc="sp")
            eq = em("scalar", lambda s: s.activation(out=Q2[:], in_=pA[:, 0, :],
                                                     func=AF.Identity),
                    waits=[("sp", cq), state["q2_free"]], inc="sa")
            bank["A0"] = ("sa", eq)

            def mkk(hp=hp):
                def f(t):
                    for ci in range(CI):
                        t.matmul(pA[:, 1, :],
                                 Wqk[:, ci, 1024 + 128 * hp:1024 + 128 * (hp + 1)],
                                 XN[:, 0, ci, :], start=(ci == 0), stop=False)
                    return t.matmul(pA[:, 1, :],
                                    brow[0:1, 1024 + 128 * hp:1024 + 128 * (hp + 1)],
                                    onestok[0:1, :], start=False, stop=True)
                return f
            ck = em("tensor", mkk(), waits=[("sv", wch_ready[8 + hp])] + bw("A1"), inc="sp")
            ek = em("scalar", lambda s: s.activation(out=K2[:], in_=pA[:, 1, :],
                                                     func=AF.Identity),
                    waits=[("sp", ck), state["k2_free"]], inc="sa")
            bank["A1"] = ("sa", ek)
            return eq, ek

        def v_batch(seg, vb):
            wensure(widx[16 + 4 * vb + 3] + 1)
            bl = ["A0", "A1", "B0", "B1"]
            ev = None
            for q4 in range(NQ):
                bn = bl[q4]

                def mkv(q4=q4, vb=vb, bn=bn):
                    def f(t):
                        for ci in range(CI):
                            t.matmul(bview[bn], XN[:, 0, ci, 128 * q4:128 * (q4 + 1)],
                                     Wv[:, ci, 512 * vb:512 * (vb + 1)],
                                     start=(ci == 0), stop=False)
                        return t.matmul(bview[bn], ones1[0:1, 0:P],
                                        brow[0:1, 2048 + 512 * vb:2048 + 512 * (vb + 1)],
                                        start=False, stop=True)
                    return f
                cv = em("tensor", mkv(),
                        waits=[("sv", state["xn_ready"][(seg, 0)]),
                               ("sv", wch_ready[16 + 4 * vb + 3]),
                               ("sb", sb_done), ("sg", sg_done)] + bw(bn), inc="sp")
                evw = [("sp", cv)] + ([state["vsb_free"],
                                       state["hs_free2"][1]] if q4 == 0 else [])
                ev = em("scalar", lambda s, q4=q4, bn=bn: s.activation(
                    out=Vsb[:, q4, :], in_=bview[bn], func=AF.Identity),
                    waits=evw, inc="sa")
                bank[bn] = ("sa", ev)
            state["vsb_ready"] = ("sa", ev)

        def scores_c01(seg, hp, eq, ek):
            def mk(base, pX):
                def f(t):
                    t.matmul(pX[:, 0, 0:512], K2[base:base + 64, 0:128],
                             Q2[base:base + 64, 0:512], start=True, stop=True,
                             tile_position=(base, 0))
                    return t.matmul(pX[:, 1, 0:384], K2[base:base + 64, 128:256],
                                    Q2[base:base + 64, 128:512], start=True, stop=True,
                                    tile_position=(base, 0))
                return f
            cse_ = em("tensor", mk(0, pA),
                      waits=[("sa", eq), ("sa", ek), ("sv", ps0)] + bw("A0", "A1"),
                      inc="sp")
            cso_ = em("tensor", mk(64, pB), waits=bw("B0", "B1"), inc="sp")
            return cse_, cso_

        def exps_a(seg, hp, cse_, cso_, avdone):
            ce = em("scalar", lambda s: s.activation(
                out=pT[:, 0, 0:1024].rearrange("p (a b) -> p a b", a=2, b=512),
                in_=pA[:, :, :], func=AF.Exp, bias=EXPB, scale=f32(0.125)),
                waits=[("sp", cse_), state["pt_free"][0]], inc="sa")
            bank["A0"] = bank["A1"] = ("sa", ce)
            co = em("scalar", lambda s: s.activation(
                out=pT[:, 1, 0:1024].rearrange("p (a b) -> p a b", a=2, b=512),
                in_=pB[:, :, :], func=AF.Exp, bias=EXPB, scale=f32(0.125)),
                waits=[("sp", cso_), state["pt_free"][1]], inc="sa")
            bank["B0"] = bank["B1"] = ("sa", co)
            return ce, co

        def scores_c23(seg, hp):
            def mk(base, pX):
                def f(t):
                    t.matmul(pX[:, 0, 0:256], K2[base:base + 64, 256:384],
                             Q2[base:base + 64, 256:512], start=True, stop=True,
                             tile_position=(base, 0))
                    return t.matmul(pX[:, 1, 0:128], K2[base:base + 64, 384:512],
                                    Q2[base:base + 64, 384:512], start=True, stop=True,
                                    tile_position=(base, 0))
                return f
            cse_ = em("tensor", mk(0, pA), waits=bw("A0", "A1"), inc="sp")
            cso_ = em("tensor", mk(64, pB), waits=bw("B0", "B1"), inc="sp")
            if hp is not None:
                state["q2_free"] = ("sp", cso_)
                state["k2_free"] = ("sp", cso_)
            return cse_, cso_

        def exps_b(seg, hp, cse_, cso_):
            ce = em("scalar", lambda s: s.activation(
                out=pT[:, 0, 1024:1536].rearrange("p (a b) -> p a b", a=2, b=256),
                in_=pA[:, :, 0:256], func=AF.Exp, bias=EXPB, scale=f32(0.125)),
                waits=[("sp", cse_)], inc="sa")
            bank["A0"] = bank["A1"] = ("sa", ce)
            co = em("scalar", lambda s: s.activation(
                out=pT[:, 1, 1024:1536].rearrange("p (a b) -> p a b", a=2, b=256),
                in_=pB[:, :, 0:256], func=AF.Exp, bias=EXPB, scale=f32(0.125)),
                waits=[("sp", cso_)], inc="sa")
            bank["B0"] = bank["B1"] = ("sa", co)
            return ce, co

        def masks(seg, hp, cea, coa, ceb, cob):
            out = []
            for pi, (ca_, cb_) in ((0, (cea, ceb)), (1, (coa, cob))):
                def mk(pi=pi):
                    def f(v):
                        ins = None
                        for c in range(4):
                            sl = pT[:, pi, PT_OFF[c]:PT_OFF[c] + 128]
                            ins = v.tensor_tensor(sl, sl, utri[:], op=ALU.mult)
                        return ins
                    return f
                out.append(em("vector", mk(), waits=[("sa", ca_), ("sa", cb_),
                                                     ("sg", sg_done)], inc="sv"))
            return out

        def av_head(seg, hp, pi, cmk):
            h = 2 * hp + pi
            obank = "Q" if pi == 0 else "K"
            ovw = oEv if pi == 0 else oOd
            vcol = 64 * h - 512 * (hp // 4)

            def mk_av(pi=pi, vcol=vcol, ovw=ovw):
                def f(t):
                    first = True
                    ins = None
                    for j in range(NQ):
                        for c in range(j + 1):
                            lt = pT[:, pi,
                                    PT_OFF[c] + 128 * (j - c):PT_OFF[c] + 128 * (j - c) + 128]
                            ins = t.matmul(ovw[:, j, 0:64], lt, Vsb[:, c, vcol:vcol + 64],
                                           start=first, stop=False)
                            first = False
                            ins = t.matmul(ovw[:, j, 64:65], lt, onescol[0:P, 0:1],
                                           start=False, stop=(j == NQ - 1 and c == j))
                    return ins
                return f
            cav = em("tensor", mk_av(),
                     waits=[("sv", cmk), state["vsb_ready"]] + bw(obank), inc="sp")
            state["pt_free"][pi] = ("sp", cav)

            crz = em("vector", lambda v, pi=pi, ovw=ovw: v.reciprocal(
                rz[:, pi, :], ovw[:, :, 64]), waits=[("sp", cav)], inc="sv")
            cno = em("vector", lambda v, pi=pi, ovw=ovw: v.tensor_tensor(
                OT[:, 0:NQ, 64 * pi:64 * pi + 64], ovw[:, :, 0:64],
                rz[:, pi, :, None].broadcast_to([P, NQ, 64]), op=ALU.mult),
                waits=[("sv", crz)] + ([state["ot_free"]] if pi == 0 else []),
                inc="sv")
            bank[obank] = ("sv", cno)
            return cno, cav

        def attn_pair_out(seg, hp, cno0, cno1):
            state["trtile"] += 1
            tb = "T%d" % (state["trtile"] % 2)

            def mk_tr(tb=tb):
                def f(t):
                    ins = None
                    for j in range(NQ):
                        ins = t.transpose(bview[tb][:, j, :], OT[:, j, :], ident[:])
                    return ins
                return f
            ctr = em("tensor", mk_tr(), waits=[("sv", cno0), ("sv", cno1)] + bw(tb),
                     inc="sp")
            state["ot_free"] = ("sp", ctr)
            evw = [("sp", ctr)] + ([state["ofm_free"]] if hp == 0 else [])
            cev = em("vector", lambda v, hp=hp, tb=tb: v.tensor_copy(
                OFM[:, hp, :], bview[tb][:, 0:NQ, :]), waits=evw, inc="sv")
            bank[tb] = ("sv", cev)
            return cev

        def attention(seg):
            pend_av = None
            pend_out = None
            last_ofm = None

            def do_av(hp, cmk_e, cmk_o):
                cne, cave = av_head(seg, hp, 0, cmk_e)
                cno, cavo = av_head(seg, hp, 1, cmk_o)
                if hp in (3, 7):
                    state["vsb_free"] = ("sp", cavo)
                    if hp == 7:
                        state["vsb_attn_done"] = ("sp", cavo)
                return cne, cno

            for hp in range(8):
                if pend_out is not None:
                    last_ofm = attn_pair_out(seg, *pend_out)
                    pend_out = None
                if hp == 4:
                    php, cmk_e, cmk_o = pend_av
                    cne, cno = do_av(php, cmk_e, cmk_o)
                    pend_out = (php, cne, cno)
                    pend_av = None
                    v_batch(seg, 1)
                eq, ek = qkv_pair(seg, hp)
                if hp == 0:
                    v_batch(seg, 0)
                cse_, cso_ = scores_c01(seg, hp, eq, ek)
                if pend_av is not None:
                    php, cmk_e, cmk_o = pend_av
                    cne, cno = do_av(php, cmk_e, cmk_o)
                    pend_out = (php, cne, cno)
                    pend_av = None
                cea, coa = exps_a(seg, hp, cse_, cso_, None)
                cse2, cso2 = scores_c23(seg, hp)
                ceb, cob = exps_b(seg, hp, cse2, cso2)
                cmk_e, cmk_o = masks(seg, hp, cea, coa, ceb, cob)
                pend_av = (hp, cmk_e, cmk_o)
                if seg == 0:
                    wpump(2)
                ln_pump(maxtr=2, maxpre=2)
            if pend_out is not None:
                last_ofm = attn_pair_out(seg, *pend_out)
            php, cmk_e, cmk_o = pend_av
            cne, cno = do_av(php, cmk_e, cmk_o)
            last_ofm = attn_pair_out(seg, php, cne, cno)
            state["ofm_ready"][seg] = last_ofm

        pending_tail = [None]

        def ffn_group(seg, r):
            rb = r % 2
            gp_ = (4 * seg + r) % 2
            HSg = HS if gp_ == 0 else Vsb[:, 0:2, :]
            xr = [("sv", state["xn_ready"][(seg, r)])]
            wf = [("swfa", swfa_done)]
            wfe = [("swfb", swfb_done)]

            def mk_rt():
                def f(t):
                    ins = None
                    for q in range(NQ):
                        for ci in range(CI):
                            ins = t.matmul(prt[:, q, :],
                                           XN[:, rb, ci, 128 * q:128 * (q + 1)],
                                           Wr[:, ci, :],
                                           start=(q == 0 and ci == 0),
                                           stop=(q == NQ - 1 and ci == CI - 1))
                    return ins
                return f
            crt = em("tensor", mk_rt(), waits=xr + wf + bw("B0"), inc="sp")
            cz0 = em("scalar", lambda s: s.activation(out=keepb[:], in_=prt[:, :, :],
                                                      func=AF.Tanh, scale=f32(0.5)),
                     waits=[("sp", crt), state["zrt_free"]], inc="sa")
            bank["B0"] = ("sa", cz0)
            cz1 = em("vector", lambda v: v.tensor_scalar(
                zrt[:], keepb[:], -1.0, 1.0, op0=ALU.mult, op1=ALU.add),
                waits=[("sa", cz0)], inc="sv")
            cz2 = em("vector", lambda v: v.reciprocal(wgt[:], zrt[:]),
                     waits=[("sv", cz1)], inc="sv")
            cz = em("vector", lambda v: v.scalar_tensor_tensor(
                out=zrt[:], in0=keepb[:], scalar=1.0, in1=wgt[:],
                op0=ALU.add, op1=ALU.mult), waits=[("sv", cz2)], inc="sv")

            cG = em("vector", lambda v: v.tensor_tensor(
                Grt[:], zrt[:, :, None, :].broadcast_to([P, NQ, E, E]),
                zrt[:, :, :, None].broadcast_to([P, NQ, E, E]), op=ALU.is_gt),
                waits=[("sv", cz)], inc="sv")
            crk = em("vector", lambda v: v.tensor_reduce(rankb[:], Grt[:], axis=AX,
                                                         op=ALU.add),
                     waits=[("sv", cG)], inc="sv")
            cwg = em("vector", lambda v: v.scalar_tensor_tensor(
                wgt[:], rankb[:], 5.5, zrt[:], op0=ALU.is_lt, op1=ALU.mult),
                     waits=[("sv", crk)], inc="sv")
            css = em("vector", lambda v: v.tensor_reduce(ssb[:], wgt[:], axis=AX,
                                                         op=ALU.add),
                     waits=[("sv", cwg)], inc="sv")
            crs = em("vector", lambda v: v.reciprocal(rsb[:], ssb[:]),
                     waits=[("sv", css)], inc="sv")

            def mk_g():
                def f(v):
                    ins = None
                    for q in range(NQ):
                        ins = v.tensor_scalar(gtm[:, q, :], wgt[:, q, :], rsb[:, q:q + 1],
                                              None, op0=ALU.mult)
                    return ins
                return f
            cg = em("vector", mk_g(), waits=[("sv", crs), state["gtm_free"]], inc="sv")
            state["zrt_free"] = ("sv", cg)
            ln_pump()

            ca = {}
            for nm, Wsx, blist in (("a1", Ws1, ("A0", "A1")), ("a2", Ws2, ("B0", "B1"))):
                for k in range(2):
                    bn = blist[k]

                    def mk_a(k=k, bn=bn, Wsx=Wsx):
                        def f(t):
                            ins = None
                            for ci in range(CI):
                                ins = t.matmul(bview[bn], Wsx[:, ci, 128 * k:128 * (k + 1)],
                                               XN[:, rb, ci, :],
                                               start=(ci == 0), stop=(ci == CI - 1))
                            return ins
                        return f
                    ca[(nm, k)] = em("tensor", mk_a(), waits=xr + wf + bw(bn), inc="sp")

            cs_ = {}
            for k in range(2):
                bn = "A%d" % k
                sw = [("sp", ca[("a1", k)])]
                if k == 0:
                    sw.append(state["hs_free2"][gp_])
                    if gp_ == 1:
                        sw.append(state.get("vsb_attn_done"))
                cs_[k] = em("scalar", lambda s, k=k, HSg=HSg: s.activation(
                    out=HSg[:, k, :], in_=pA[:, k, :], func=AF.Silu),
                    waits=sw, inc="sa")
                bank[bn] = ("sa", cs_[k])
            ch = {}
            for k in range(2):
                bn = "B%d" % k
                ch[k] = em("vector", lambda v, k=k, HSg=HSg: v.tensor_tensor(
                    HSg[:, k, :], HSg[:, k, :], pB[:, k, :], op=ALU.mult),
                    waits=[("sa", cs_[k]), ("sp", ca[("a2", k)])], inc="sv")
                bank[bn] = ("sv", ch[k])

            if pending_tail[0] is not None:
                pending_tail[0]()
                pending_tail[0] = None

            pQg = pQ[:, 0:512].rearrange("p (q t) -> p q t", q=NQ, t=P)

            def mk_gt():
                def f(t):
                    ins = None
                    for q in range(NQ):
                        ins = t.transpose(pQg[0:E, q, 0:64].bitcast(BF16), gtm[:, q, :],
                                          ident[:])
                    return ins
                return f
            ctg = em("tensor", mk_gt(), waits=[("sv", cg)] + bw("Q"), inc="sp")
            state["gtm_free"] = ("sp", ctg)
            cgt = em("vector", lambda v: v.tensor_copy(
                gT[:, :, :], pQg[0:E, 0:NQ, 0:64].bitcast(BF16)),
                waits=[("sp", ctg), state["gt_free"]], inc="sv")
            bank["Q"] = ("sv", cgt)
            ln_pump()

            chp_last = None
            last_chp2 = None
            pend_gb = [None]

            def emit_gb(e0, e1, cu):
                cgb0 = em("tensor", lambda t, e=e0: t.matmul(
                    pQ[:, :], sel[0:E, P * e:P * (e + 1)],
                    gT[0:E, :, :].rearrange("a b c -> a (b c)"),
                    start=True, stop=True), waits=[("sv", cgt)] + bw("Q"), inc="sp")
                cgb1 = em("tensor", lambda t, e=e1: t.matmul(
                    pK[:, :], sel[0:E, P * e:P * (e + 1)],
                    gT[0:E, :, :].rearrange("a b c -> a (b c)"),
                    start=True, stop=True), waits=[("sv", cgt)] + bw("K"), inc="sp")
                chp0 = em("vector", lambda v, e=e0: v.tensor_tensor(
                    HP[:, e, :], HP[:, e, :], pQ[:, :], op=ALU.mult),
                    waits=[("sv", cu), ("sp", cgb0)], inc="sv")
                bank["Q"] = ("sv", chp0)
                chp1 = em("vector", lambda v, e=e1: v.tensor_tensor(
                    HP[:, e, :], HP[:, e, :], pK[:, :], op=ALU.mult),
                    waits=[("sp", cgb1)], inc="sv")
                bank["K"] = ("sv", chp1)
                return chp1

            for ep in range(E // 2):
                e0, e1 = 2 * ep, 2 * ep + 1

                def mk_w(Wx, bn, e):
                    def f(t):
                        ins = None
                        for ci in range(CI):
                            ins = t.matmul(bview[bn], Wx[:, e, ci, :],
                                           XN[:, rb, ci, :],
                                           start=(ci == 0), stop=(ci == CI - 1))
                        return ins
                    return f
                w2pair = ("B0", "B1")
                w2view = pB
                cw10 = em("tensor", mk_w(W1, "A0", e0), waits=wfe + bw("A0"), inc="sp")
                cw11 = em("tensor", mk_w(W1, "A1", e1), waits=wfe + bw("A1"), inc="sp")
                cw20 = em("tensor", mk_w(W2, "B0", e0), waits=wfe + bw("B0"), inc="sp")
                cw21 = em("tensor", mk_w(W2, "B1", e1), waits=wfe + bw("B1"), inc="sp")
                if pend_gb[0] is not None:
                    chp_last = emit_gb(*pend_gb[0])
                    pend_gb[0] = None
                cse = em("scalar", lambda s: s.activation(
                    out=s1e[:, :, :], in_=pA[:, :, :], func=AF.Silu),
                    waits=[("sp", cw10), ("sp", cw11), last_chp2], inc="sa")
                bank["A0"] = bank["A1"] = ("sa", cse)
                cu = em("vector", lambda v, e0=e0: v.tensor_tensor(
                    HP[:, e0:e0 + 2, :], s1e[:, :, :], pB[:, :, :], op=ALU.mult),
                    waits=[("sa", cse), ("sp", cw20), ("sp", cw21)] +
                          ([state["hp_free"]] if ep == 0 else []), inc="sv")
                bank["B0"] = bank["B1"] = ("sv", cu)
                pend_gb[0] = (e0, e1, cu)
                last_chp2 = ("sv", cu)
                ln_pump()
            chp_last = emit_gb(*pend_gb[0])
            pend_gb[0] = None

            def tail(seg=seg, r=r, chp_last=chp_last, ch=ch, HSg=HSg, gp_=gp_):
                fin_last = None
                for q in range(NQ):
                    for Hh in range(2):
                        bn = "Q" if Hh == 0 else "K"

                        def mk_fin(q=q, Hh=Hh, bn=bn, r=r):
                            def f(t):
                                ops = []
                                for k in range(2):
                                    ops.append((HSg[:, k, 128 * q:128 * (q + 1)],
                                                Ws3[:, k, 512 * Hh:512 * (Hh + 1)]))
                                for e in range(E):
                                    ops.append((HP[:, e, 128 * q:128 * (q + 1)],
                                                W3[:, e, 512 * Hh:512 * (Hh + 1)]))
                                if r == 0:
                                    for ci in range(CI):
                                        ops.append((OFM[:, ci, 128 * q:128 * (q + 1)],
                                                    Wo[:, ci, 512 * Hh:512 * (Hh + 1)]))
                                    ops.append((ones1[0:1, 0:P],
                                                brow[0:1, 3 * C + 512 * Hh:
                                                     3 * C + 512 * (Hh + 1)]))
                                ins = None
                                for i, (lt, rh) in enumerate(ops):
                                    ins = t.matmul(bview[bn], lt, rh,
                                                   start=(i == 0), stop=(i == len(ops) - 1))
                                return ins
                            return f
                        fw = [("sv", chp_last), ("sv", ch[0]), ("sv", ch[1]),
                              ("swfc", swfc_done)] + bw(bn)
                        if r == 0:
                            fw += [("sv", state["ofm_ready"][seg]), ("sv", wch_ready[31]),
                                   ("sb", sb_done)]
                        cf = em("tensor", mk_fin(), waits=fw, inc="sp")
                        fin_last = cf
                        sl4 = state["fin_slot"] % 4
                        state["fin_slot"] += 1
                        dst = (out_sb[:, sl4, :] if sl4 < 2
                               else WinNat[:, sl4 - 2, :].bitcast(F32))
                        ew = [("sp", cf), ("sst%d" % sl4, 16 * state["st_uses"][sl4])]
                        if sl4 >= 2 and state["st_uses"][sl4] == 0:
                            ew.append(tr_read[sl4 - 2])
                        ce = em("scalar", lambda s, dst=dst, bn=bn: s.activation(
                            out=dst, in_=bview[bn], func=AF.Identity),
                            waits=ew, inc="sa")
                        bank[bn] = ("sa", ce)
                        state["acc_uses"][sl4] += 1
                        em("gpsimd", lambda g, seg=seg, r=r, q=q, Hh=Hh, dst=dst: g.dma_start(
                            out=dst,
                            in_=xv[seg, r, q][:, 512 * Hh:512 * (Hh + 1)],
                            accum_op=ALU.add),
                            waits=[("sa", ce)], inc="sacc%d" % sl4, k=16)
                        em("sync", lambda sy, seg=seg, r=r, q=q, Hh=Hh, dst=dst: sy.dma_start(
                            out=ov[seg, r, q][:, 512 * Hh:512 * (Hh + 1)],
                            in_=dst),
                            waits=[("sacc%d" % sl4, 16 * state["acc_uses"][sl4])],
                            inc="sst%d" % sl4, k=16)
                        state["st_uses"][sl4] += 1
                    ln_pump()
                state["hp_free"] = ("sp", fin_last)
                state["hs_free2"][gp_] = ("sp", fin_last)
                state["gt_free"] = ("sp", fin_last)
                if r == 0:
                    state["ofm_free"] = ("sp", fin_last)
                state["xnbuf_free"][rb] = ("sp", fin_last)
                ffn_done.add((seg, r))

            pending_tail[0] = tail

        def flush_tail():
            if pending_tail[0] is not None:
                pending_tail[0]()
                pending_tail[0] = None

        wensure(10)
        ensure_group(0, 0)
        attention(0)
        wensure(len(witems))
        for seg in range(NSEG):
            for r in range(NR):
                ensure_group(seg, r)
                ffn_group(seg, r)
                if seg == 0 and r == NR - 1:
                    flush_tail()
                    ensure_group(1, 0)
                    attention(1)
        flush_tail()

        em("sync", lambda sy: sy.nop(),
           waits=[("sst%d" % i, 16 * state["st_uses"][i]) for i in range(4)])

        ninstr = {e: len(sch[e]) for e in ENGS}

        with nc.Block() as block:
            @block.sync
            def _(eng):
                for fn in sch["sync"]:
                    fn(eng)

            @block.gpsimd
            def _(eng):
                for fn in sch["gpsimd"]:
                    fn(eng)

            @block.tensor
            def _(eng):
                for fn in sch["tensor"]:
                    fn(eng)

            @block.vector
            def _(eng):
                for fn in sch["vector"]:
                    fn(eng)

            @block.scalar
            def _(eng):
                for fn in sch["scalar"]:
                    fn(eng)

    return nc, sbuf_left, ninstr


def kernel(**inputs):
    xfull = np.ascontiguousarray(inputs["L0_attn_x"])
    nc, _, _ = build_nc()
    base = {
        "w_in": np.ascontiguousarray(inputs["L0_attn_w_in"]),
        "b_in": np.ascontiguousarray(inputs["L0_attn_b_in"]).reshape(1, -1),
        "w_out": np.ascontiguousarray(inputs["L0_attn_w_out"]),
        "b_out": np.ascontiguousarray(inputs["L0_attn_b_out"]).reshape(1, -1),
        "w1": np.ascontiguousarray(inputs["L0_ffn_w1"]),
        "w2": np.ascontiguousarray(inputs["L0_ffn_w2"]),
        "w3": np.ascontiguousarray(inputs["L0_ffn_w3"]),
        "ws1": np.ascontiguousarray(inputs["L0_ffn_ws1"]),
        "ws2": np.ascontiguousarray(inputs["L0_ffn_ws2"]),
        "ws3": np.ascontiguousarray(inputs["L0_ffn_ws3"]),
        "wr": np.ascontiguousarray(inputs["L0_ffn_wr"]),
    }
    in_maps = []
    for k in range(8):
        b, half = k // 2, k % 2
        m = dict(base)
        m["x"] = np.ascontiguousarray(xfull[b, half * 4096:(half + 1) * 4096])
        in_maps.append(m)
    res = run_bass_kernel_spmd(nc, in_maps, core_ids=list(range(8)))
    out = np.empty((4, 8192, 1024), np.float32)
    for k in range(8):
        b, half = k // 2, k % 2
        out[b, half * 4096:(half + 1) * 4096] = res.results[k]["out"]
    return out


if __name__ == "__main__":
    nc, left, ni = build_nc()
    print("sbuf bytes remaining:", left)
    print("instr counts:", ni)


# version 30 (final)
# speedup vs baseline: 1.0116x; 1.0116x over previous
import contextlib

import numpy as np

import concourse.bass as bass
import concourse.mybir as mybir
from concourse.bass_utils import run_bass_kernel_spmd

F32 = mybir.dt.float32
BF16 = mybir.dt.bfloat16
AF = mybir.ActivationFunctionType
ALU = mybir.AluOpType
AX = mybir.AxisListType.X
f32 = np.float32

P = 128
C = 1024
CI = 8
NSEG = 2
NR = 4
NQ = 4
NT = NR * NQ
E = 8
ISH = 256
EXPB = f32(-8.0)
EPS = f32(1e-5)
PT_OFF = [0, 512, 1024, 1280]

ENGS = ("sync", "gpsimd", "tensor", "vector", "scalar")


def build_nc():
    nc = bass.Bass("TRN2", target_bir_lowering=False)

    dx = nc.dram_tensor("x", [NSEG * 2048, C], F32, kind="ExternalInput").ap()
    dout = nc.dram_tensor("out", [NSEG * 2048, C], F32, kind="ExternalOutput").ap()
    dwin = nc.dram_tensor("w_in", [3 * C, C], F32, kind="ExternalInput").ap()
    dbin = nc.dram_tensor("b_in", [1, 3 * C], F32, kind="ExternalInput").ap()
    dwout = nc.dram_tensor("w_out", [C, C], F32, kind="ExternalInput").ap()
    dbout = nc.dram_tensor("b_out", [1, C], F32, kind="ExternalInput").ap()
    dw1 = nc.dram_tensor("w1", [E, C, 128], F32, kind="ExternalInput").ap()
    dw2 = nc.dram_tensor("w2", [E, C, 128], F32, kind="ExternalInput").ap()
    dw3 = nc.dram_tensor("w3", [E, 128, C], F32, kind="ExternalInput").ap()
    dws1 = nc.dram_tensor("ws1", [C, ISH], F32, kind="ExternalInput").ap()
    dws2 = nc.dram_tensor("ws2", [C, ISH], F32, kind="ExternalInput").ap()
    dws3 = nc.dram_tensor("ws3", [ISH, C], F32, kind="ExternalInput").ap()
    dwr = nc.dram_tensor("wr", [C, E], F32, kind="ExternalInput").ap()

    xv = dx.rearrange("(s q p r) c -> s r q p c", s=NSEG, q=NQ, p=P, r=NR)
    ov = dout.rearrange("(s q p r) c -> s r q p c", s=NSEG, q=NQ, p=P, r=NR)

    ctx = contextlib.ExitStack()
    with ctx:
        A = ctx.enter_context

        Wqk = A(nc.sbuf_tensor("Wqk", [P, CI, 2048], BF16))
        Wv = A(nc.sbuf_tensor("Wv", [P, CI, 1024], BF16))
        Wo = A(nc.sbuf_tensor("Wo", [P, CI, 1024], BF16))
        W1 = A(nc.sbuf_tensor("W1", [P, E, CI, 128], BF16))
        W2 = A(nc.sbuf_tensor("W2", [P, E, CI, 128], BF16))
        W3 = A(nc.sbuf_tensor("W3", [P, E, 1024], BF16))
        Ws1 = A(nc.sbuf_tensor("Ws1", [P, CI, ISH], BF16))
        Ws2 = A(nc.sbuf_tensor("Ws2", [P, CI, ISH], BF16))
        Ws3 = A(nc.sbuf_tensor("Ws3", [P, 2, 1024], BF16))
        Wr = A(nc.sbuf_tensor("Wr", [P, CI, E], BF16))
        WinNat = A(nc.sbuf_tensor("WinNat", [P, 2, 1024], BF16))
        brow = A(nc.sbuf_tensor("brow", [1, 4 * C], BF16))

        ident = A(nc.sbuf_tensor("ident", [P, P], BF16))
        utri = A(nc.sbuf_tensor("utri", [P, P], BF16))
        ones1 = A(nc.sbuf_tensor("ones1", [1, P], BF16))
        onestok = A(nc.sbuf_tensor("onestok", [1, 512], BF16))
        onescol = A(nc.sbuf_tensor("onescol", [P, 1], BF16))

        XF = A(nc.sbuf_tensor("XF", [P, 2, C], F32))
        xn_tm = A(nc.sbuf_tensor("xn_tm", [P, 2, C], BF16))
        XN = A(nc.sbuf_tensor("XN", [P, 2, CI, 512], BF16))
        st_s = A(nc.sbuf_tensor("st_s", [P, NSEG, NT], F32))
        st_ss = A(nc.sbuf_tensor("st_ss", [P, NSEG, NT], F32))
        st_m = A(nc.sbuf_tensor("st_m", [P, NSEG, NT], F32))
        st_m2 = A(nc.sbuf_tensor("st_m2", [P, NSEG, NT], F32))
        st_v = A(nc.sbuf_tensor("st_v", [P, NSEG, NT], F32))
        st_r = A(nc.sbuf_tensor("st_r", [P, NSEG, NT], F32))

        Q2 = A(nc.sbuf_tensor("Q2", [P, 512], BF16))
        K2 = A(nc.sbuf_tensor("K2", [P, 512], BF16))
        Vsb = A(nc.sbuf_tensor("Vsb", [P, NQ, 512], BF16))
        pT = A(nc.sbuf_tensor("pT", [P, 2, 1536], BF16))
        OT = A(nc.sbuf_tensor("OT", [P, NQ, P], BF16))
        OFM = A(nc.sbuf_tensor("OFM", [P, CI, 512], BF16))
        rz = A(nc.sbuf_tensor("rz", [P, 2, NQ], F32))

        zrt = A(nc.sbuf_tensor("zrt", [P, NQ, E], F32))
        Grt = A(nc.sbuf_tensor("Grt", [P, NQ, E, E], BF16))
        rankb = A(nc.sbuf_tensor("rankb", [P, NQ, E], F32))
        keepb = A(nc.sbuf_tensor("keepb", [P, NQ, E], F32))
        wgt = A(nc.sbuf_tensor("wgt", [P, NQ, E], F32))
        ssb = A(nc.sbuf_tensor("ssb", [P, NQ], F32))
        rsb = A(nc.sbuf_tensor("rsb", [P, NQ], F32))
        gtm = A(nc.sbuf_tensor("gtm", [P, NQ, E], BF16))
        gT = A(nc.sbuf_tensor("gT", [E, NQ, P], BF16))
        HS = A(nc.sbuf_tensor("HS", [P, 2, 512], BF16))
        s1e = A(nc.sbuf_tensor("s1e", [P, 2, 512], BF16))
        HP = A(nc.sbuf_tensor("HP", [P, E, 512], BF16))
        out_sb = A(nc.sbuf_tensor("osb", [P, 2, 512], F32))
        sel = A(nc.sbuf_tensor("sel", [E, E * P], BF16))

        ptT0 = A(nc.psum_tensor("ptT0", [P, CI, P], BF16))
        ptT1 = A(nc.psum_tensor("ptT1", [P, CI, P], BF16))
        pQ = A(nc.psum_tensor("pQ", [P, 512], F32))
        pK = A(nc.psum_tensor("pK", [P, 512], F32))
        pA = A(nc.psum_tensor("pA", [P, 2, 512], F32))
        pB = A(nc.psum_tensor("pB", [P, 2, 512], F32))
        oEv = pQ[:, 0:272].rearrange("p (j k) -> p j k", j=NQ, k=68)
        oOd = pK[:, 0:272].rearrange("p (j k) -> p j k", j=NQ, k=68)
        prt = pB[:, 0, 0:32].rearrange("p (j e) -> p j e", j=NQ, e=E)

        sbuf_left = nc.sbuf_bytes_remaining if isinstance(nc.sbuf_bytes_remaining, int) else nc.sbuf_bytes_remaining()

        sem_names = ["sg", "sp", "sv", "sa", "sb", "swfa", "swfb", "swfc", "swin0", "swin1",
                     "sx0", "sx1", "sacc0", "sacc1", "sacc2", "sacc3",
                     "sst0", "sst1", "sst2", "sst3"]
        S = {n: A(nc.semaphore(n)) for n in sem_names}
        cnt = {n: 0 for n in sem_names}
        sch = {e: [] for e in ENGS}

        def em(eng, fn, waits=(), inc=None, k=1):
            wd = {}
            for item in waits:
                if item is None:
                    continue
                s, v = item
                if v and v > wd.get(s, 0):
                    wd[s] = v
            for s, v in wd.items():
                assert v <= cnt[s], f"future wait {s} {v} > {cnt[s]}"
            wt = tuple(wd.items())

            def run(eng_obj, wt=wt, fn=fn, inc=inc, k=k):
                for s, v in wt:
                    eng_obj.wait_ge(S[s], v)
                ins = fn(eng_obj)
                if inc is not None:
                    ins.then_inc(S[inc], k)

            sch[eng].append(run)
            if inc is not None:
                cnt[inc] += k
                return cnt[inc]
            return None

        bank = {n: None for n in ["T0", "T1", "Q", "K", "A0", "A1", "B0", "B1"]}
        bview = {"T0": ptT0[:, :, :], "T1": ptT1[:, :, :], "Q": pQ[:, :], "K": pK[:, :],
                 "A0": pA[:, 0, :], "A1": pA[:, 1, :],
                 "B0": pB[:, 0, :], "B1": pB[:, 1, :]}

        def bw(*names):
            return [bank[n] for n in names if bank[n] is not None]

        c1 = em("vector", lambda v: v.memset(ident[:], 1.0), inc="sv")
        c2 = em("vector", lambda v: v.memset(utri[:], 1.0), inc="sv")
        em("gpsimd", lambda g: g.affine_select(
            ident[:], ident[:], pattern=[[1, P]], compare_op=ALU.is_equal,
            fill=0.0, base=0, channel_multiplier=-1), waits=[("sv", c1)], inc="sg")
        em("gpsimd", lambda g: g.affine_select(
            utri[:], utri[:], pattern=[[1, P]], compare_op=ALU.is_ge,
            fill=0.0, base=0, channel_multiplier=-1), waits=[("sv", c2)], inc="sg")
        em("vector", lambda v: v.memset(ones1[:], 1.0), inc="sg")
        em("vector", lambda v: v.memset(onestok[:], 1.0), inc="sg")
        c3 = em("vector", lambda v: v.memset(onescol[:], 1.0), inc="sg")
        c4 = em("vector", lambda v: v.memset(sel[:], 1.0), inc="sg")
        c5 = em("gpsimd", lambda g: g.affine_select(
            sel[:], sel[:], pattern=[[1, E * P]], compare_op=ALU.is_ge,
            fill=0.0, base=0, channel_multiplier=-P), waits=[("sg", c4)], inc="sg")
        sg_done = em("gpsimd", lambda g: g.affine_select(
            sel[:], sel[:], pattern=[[-1, E * P]], compare_op=ALU.is_ge,
            fill=0.0, base=P - 1, channel_multiplier=P), waits=[("sg", c5)], inc="sg")

        em("vector", lambda v: v.memset(pQ[:], 0.0), inc="sv")
        em("vector", lambda v: v.memset(pK[:], 0.0), inc="sv")
        em("vector", lambda v: v.memset(pA[:], 0.0), inc="sv")
        ps0 = em("vector", lambda v: v.memset(pB[:], 0.0), inc="sv")

        em("gpsimd", lambda g: g.dma_start(out=brow[:, 0:3 * C], in_=dbin), inc="sb", k=16)
        sb_done = em("gpsimd", lambda g: g.dma_start(out=brow[:, 3 * C:], in_=dbout),
                     inc="sb", k=16)

        wch_ready = {}
        tr_read = {0: None, 1: None}
        worder = [0, 8, 16, 17, 18, 19, 1, 9, 2, 10, 3, 11, 20, 21, 22, 23,
                  4, 12, 5, 13, 6, 14, 7, 15] + list(range(24, 32))
        widx_chunk = {}
        witems = []

        def _mk_chunk_item(m):
            def item():
                mi = wmi[0]
                wmi[0] += 1
                sl = mi % 2
                src_ = (dwin[128 * m:128 * (m + 1), :] if m < 24
                        else dwout[128 * (m - 24):128 * (m - 23), :])
                em("gpsimd",
                   lambda g, sl=sl, src_=src_: g.dma_start(out=WinNat[:, sl, :], in_=src_),
                   waits=[tr_read[sl]], inc="swin%d" % sl, k=16)
                tb = "T%d" % sl

                def mk_wtr(sl=sl, tb=tb):
                    def f(t):
                        ins = None
                        for ci in range(CI):
                            ins = t.transpose(bview[tb][:, ci, :],
                                              WinNat[:, sl, 128 * ci:128 * (ci + 1)],
                                              ident[:])
                        return ins
                    return f
                tr = em("tensor", mk_wtr(),
                        waits=[("swin%d" % sl, 16 * (mi // 2 + 1)), ("sg", sg_done)] + bw(tb),
                        inc="sp")
                tr_read[sl] = ("sp", tr)
                if m < 16:
                    dst = Wqk[:, :, 128 * m:128 * (m + 1)]
                elif m < 24:
                    dst = Wv[:, :, 128 * (m - 16):128 * (m - 15)]
                else:
                    dst = Wo[:, :, 128 * (m - 24):128 * (m - 23)]
                ev = em("vector",
                        lambda v, tb=tb, dst=dst: v.tensor_copy(dst, bview[tb][:, :, :]),
                        waits=[("sp", tr)], inc="sv")
                bank[tb] = ("sv", ev)
                wch_ready[m] = ev
            return item

        def _mk_ffn_item(fn, sem):
            return lambda: em("gpsimd", fn, inc=sem, k=16)

        wmi = [0]
        _ffn_shared = [
            _mk_ffn_item(lambda g: g.dma_start(
                out=Wr[:], in_=dwr.rearrange("(ci p) e -> p ci e", p=P)), "swfa"),
            _mk_ffn_item(lambda g: g.dma_start(
                out=Ws1[:], in_=dws1.rearrange("(ci p) i -> p ci i", p=P)), "swfa"),
            _mk_ffn_item(lambda g: g.dma_start(
                out=Ws2[:], in_=dws2.rearrange("(ci p) i -> p ci i", p=P)), "swfa"),
            _mk_ffn_item(lambda g: g.dma_start(
                out=Ws3[:], in_=dws3.rearrange("(k p) c -> p k c", p=P)), "swfa"),
        ]
        _ffn_exp = []
        for e in range(E):
            _ffn_exp.append(_mk_ffn_item(lambda g, e=e: g.dma_start(
                out=W1[:, e], in_=dw1[e].rearrange("(ci p) i -> p ci i", p=P)), "swfb"))
            _ffn_exp.append(_mk_ffn_item(lambda g, e=e: g.dma_start(
                out=W2[:, e], in_=dw2[e].rearrange("(ci p) i -> p ci i", p=P)), "swfb"))
        _ffn_w3 = [_mk_ffn_item(lambda g: g.dma_start(
            out=W3[:], in_=dw3.rearrange("e p c -> p e c")), "swfc")]

        _chunks = [_mk_chunk_item(m) for m in worder[:24]]
        _wo = [_mk_chunk_item(m) for m in worder[24:]]
        witems = (_chunks[0:6] + _ffn_shared + _chunks[6:8] + _ffn_exp[0:2]
                  + _chunks[8:10] + _ffn_exp[2:4] + _chunks[10:12] + _ffn_exp[4:6]
                  + _chunks[12:16] + _ffn_exp[6:8] + _chunks[16:18] + _ffn_exp[8:10]
                  + _chunks[18:20] + _ffn_exp[10:12] + _chunks[20:22] + _ffn_exp[12:14]
                  + _chunks[22:24] + _ffn_exp[14:16] + _ffn_w3 + _wo)
        for i, m in enumerate(worder):
            pass
        widx = {}
        _pos = {id(c): None for c in _chunks}
        for pos, it in enumerate(witems):
            for ci_, m in zip(_chunks + _wo, worder):
                if it is ci_:
                    widx[m] = pos
        swfa_done = 16 * 4
        swfb_done = 16 * 16
        swfc_done = 16 * 1
        wptr = [0]
        wptr = [0]

        def wensure(n):
            while wptr[0] < n:
                witems[wptr[0]]()
                wptr[0] += 1

        def wpump(n=2):
            wensure(min(wptr[0] + n, len(witems)))

        state = {
            "xf_free": {0: [None], 1: [None]},
            "xf_uses": {0: 0, 1: 0},
            "xn_ready": {},
            "xnbuf_free": {0: None, 1: None},
            "xntm_free": {0: None, 1: None},
            "ofm_ready": {},
            "ofm_free": None, "hp_free": None,
            "hs_free2": {0: None, 1: None}, "vsb_attn_done": None,
            "gt_free": None, "zrt_free": None, "gtm_free": None,
            "ot_free": None, "vsb_free": None, "q2_free": None, "k2_free": None,
            "pt_free": {0: None, 1: None},
            "acc_uses": {0: 0, 1: 0, 2: 0, 3: 0},
            "st_uses": {0: 0, 1: 0, 2: 0, 3: 0}, "fin_slot": 0,
            "tile": 0, "trtile": 0, "vsb_ready": None,
        }

        lninfo = {}

        def ln_pre(seg, j):
            r, q = j // NQ, j % NQ
            gslot = state["tile"] % 2
            xsl = state["tile"] % 2
            state["tile"] += 1
            sxn = "sx%d" % gslot
            state["xf_uses"][gslot] += 1
            uses = state["xf_uses"][gslot]
            em("sync", lambda sy, seg=seg, r=r, q=q, gslot=gslot:
               sy.dma_start(out=XF[:, gslot, :], in_=xv[seg, r, q]),
               waits=state["xf_free"][gslot], inc=sxn, k=16)
            ldw = [(sxn, 16 * uses)]
            crs = em("vector", lambda v, seg=seg, j=j, gslot=gslot: v.reduce_sum(
                st_s[:, seg, j:j + 1], XF[:, gslot, :], axis=AX), waits=ldw, inc="sv")
            csq = em("scalar", lambda s, seg=seg, j=j, gslot=gslot, xsl=xsl: s.activation(
                out=xn_tm[:, xsl, :], in_=XF[:, gslot, :], func=AF.Square,
                accum_out=st_ss[:, seg, j:j + 1]),
                waits=ldw + [state["xntm_free"][xsl]], inc="sa")
            cm = em("scalar", lambda s, seg=seg, j=j: s.activation(
                out=st_m[:, seg, j:j + 1], in_=st_s[:, seg, j:j + 1], func=AF.Identity,
                scale=f32(1.0 / C)), waits=[("sv", crs)], inc="sa")
            cm2 = em("scalar", lambda s, seg=seg, j=j: s.activation(
                out=st_m2[:, seg, j:j + 1], in_=st_m[:, seg, j:j + 1], func=AF.Square),
                waits=[("sa", cm)], inc="sa")
            cv = em("vector", lambda v, seg=seg, j=j: v.scalar_tensor_tensor(
                out=st_v[:, seg, j:j + 1], in0=st_ss[:, seg, j:j + 1], scalar=1.0 / C,
                in1=st_m2[:, seg, j:j + 1], op0=ALU.mult, op1=ALU.subtract),
                waits=[("sa", cm2)], inc="sv")
            cl = em("scalar", lambda s, seg=seg, j=j: s.activation(
                out=st_v[:, seg, j:j + 1], in_=st_v[:, seg, j:j + 1], func=AF.Ln,
                bias=EPS), waits=[("sv", cv)], inc="sa")
            cr = em("scalar", lambda s, seg=seg, j=j: s.activation(
                out=st_r[:, seg, j:j + 1], in_=st_v[:, seg, j:j + 1], func=AF.Exp,
                scale=f32(-0.5)), waits=[("sa", cl)], inc="sa")
            cxn = em("vector", lambda v, seg=seg, j=j, gslot=gslot, xsl=xsl: v.tensor_scalar(
                out=xn_tm[:, xsl, :], in0=XF[:, gslot, :],
                scalar1=st_m[:, seg, j:j + 1], scalar2=st_r[:, seg, j:j + 1],
                op0=ALU.subtract, op1=ALU.mult), waits=[("sa", cr)], inc="sv")
            state["xf_free"][gslot] = [("sv", cxn), ("sa", csq)]
            lninfo[(seg, j)] = {"cxn": cxn, "xsl": xsl}

        def ln_tr(seg, j):
            r, q = j // NQ, j % NQ
            info = lninfo[(seg, j)]
            xsl = info["xsl"]
            state["trtile"] += 1
            tb = "T%d" % (state["trtile"] % 2)

            def mk_tr(xsl=xsl, tb=tb):
                def f(t):
                    ins = None
                    for ci in range(CI):
                        ins = t.transpose(bview[tb][:, ci, :],
                                          xn_tm[:, xsl, 128 * ci:128 * (ci + 1)], ident[:])
                    return ins
                return f
            ctr = em("tensor", mk_tr(), waits=[("sv", info["cxn"])] + bw(tb),
                     inc="sp")
            state["xntm_free"][xsl] = ("sp", ctr)
            rb = r % 2
            evw = [("sp", ctr)] + ([state["xnbuf_free"][rb]] if q == 0 else [])
            cev = em("vector", lambda v, rb=rb, q=q, tb=tb: v.tensor_copy(
                XN[:, rb, :, 128 * q:128 * (q + 1)], bview[tb][:, :, :]),
                waits=evw, inc="sv")
            bank[tb] = ("sv", cev)
            if q == NQ - 1:
                state["xn_ready"][(seg, r)] = cev

        lnq = [(s_, j_) for s_ in range(NSEG) for j_ in range(NT)]
        lnp = {"pre": 0, "tr": 0}
        ffn_done = set()
        prev_user = {}
        for bufid in (0, 1):
            groups = [(s_, r_) for s_ in range(NSEG) for r_ in range(NR) if r_ % 2 == bufid]
            for i_, g_ in enumerate(groups):
                prev_user[g_] = groups[i_ - 1] if i_ > 0 else None

        def can_tr(i):
            if i >= len(lnq) or lnp["pre"] <= i + 1:
                return False
            seg, j = lnq[i]
            pu = prev_user[(seg, j // NQ)]
            return pu is None or pu in ffn_done

        def can_pre(i):
            return i < len(lnq) and (i < 2 or lnp["tr"] >= i - 1)

        def ln_pump(maxtr=1, maxpre=1):
            ntr = npre = 0
            moved = True
            while moved:
                moved = False
                if ntr < maxtr and can_tr(lnp["tr"]):
                    ln_tr(*lnq[lnp["tr"]])
                    lnp["tr"] += 1
                    ntr += 1
                    moved = True
                if npre < maxpre and can_pre(lnp["pre"]):
                    ln_pre(*lnq[lnp["pre"]])
                    lnp["pre"] += 1
                    npre += 1
                    moved = True

        def ln_pump_tr():
            if can_tr(lnp["tr"]):
                ln_tr(*lnq[lnp["tr"]])
                lnp["tr"] += 1

        def ensure_group(seg, r):
            need = (seg * NT + r * NQ) + NQ
            while lnp["tr"] < need:
                before = (lnp["tr"], lnp["pre"])
                if can_pre(lnp["pre"]):
                    ln_pre(*lnq[lnp["pre"]])
                    lnp["pre"] += 1
                if lnp["tr"] < len(lnq) and lnp["pre"] > lnp["tr"]:
                    seg_, j_ = lnq[lnp["tr"]]
                    pu = prev_user[(seg_, j_ // NQ)]
                    if pu is None or pu in ffn_done:
                        ln_tr(seg_, j_)
                        lnp["tr"] += 1
                assert (lnp["tr"], lnp["pre"]) != before, \
                    f"LN pump stuck at {before} needing {need}"

        def qkv_pair(seg, hp):
            wensure(max(widx[hp], widx[8 + hp]) + 1)
            def mkq(hp=hp):
                def f(t):
                    for ci in range(CI):
                        t.matmul(pA[:, 0, :], Wqk[:, ci, 128 * hp:128 * (hp + 1)],
                                 XN[:, 0, ci, :], start=(ci == 0), stop=False)
                    return t.matmul(pA[:, 0, :], brow[0:1, 128 * hp:128 * (hp + 1)],
                                    onestok[0:1, :], start=False, stop=True)
                return f
            cq = em("tensor", mkq(),
                    waits=[("sv", state["xn_ready"][(seg, 0)]), ("sv", wch_ready[hp]),
                           ("sb", sb_done), ("sg", sg_done)] + bw("A0"), inc="sp")
            eq = em("scalar", lambda s: s.activation(out=Q2[:], in_=pA[:, 0, :],
                                                     func=AF.Identity),
                    waits=[("sp", cq), state["q2_free"]], inc="sa")
            bank["A0"] = ("sa", eq)

            def mkk(hp=hp):
                def f(t):
                    for ci in range(CI):
                        t.matmul(pA[:, 1, :],
                                 Wqk[:, ci, 1024 + 128 * hp:1024 + 128 * (hp + 1)],
                                 XN[:, 0, ci, :], start=(ci == 0), stop=False)
                    return t.matmul(pA[:, 1, :],
                                    brow[0:1, 1024 + 128 * hp:1024 + 128 * (hp + 1)],
                                    onestok[0:1, :], start=False, stop=True)
                return f
            ck = em("tensor", mkk(), waits=[("sv", wch_ready[8 + hp])] + bw("A1"), inc="sp")
            ek = em("scalar", lambda s: s.activation(out=K2[:], in_=pA[:, 1, :],
                                                     func=AF.Identity),
                    waits=[("sp", ck), state["k2_free"]], inc="sa")
            bank["A1"] = ("sa", ek)
            return eq, ek

        def v_batch(seg, vb):
            wensure(widx[16 + 4 * vb + 3] + 1)
            bl = ["A0", "A1", "B0", "B1"]
            ev = None
            for q4 in range(NQ):
                bn = bl[q4]

                def mkv(q4=q4, vb=vb, bn=bn):
                    def f(t):
                        for ci in range(CI):
                            t.matmul(bview[bn], XN[:, 0, ci, 128 * q4:128 * (q4 + 1)],
                                     Wv[:, ci, 512 * vb:512 * (vb + 1)],
                                     start=(ci == 0), stop=False)
                        return t.matmul(bview[bn], ones1[0:1, 0:P],
                                        brow[0:1, 2048 + 512 * vb:2048 + 512 * (vb + 1)],
                                        start=False, stop=True)
                    return f
                cv = em("tensor", mkv(),
                        waits=[("sv", state["xn_ready"][(seg, 0)]),
                               ("sv", wch_ready[16 + 4 * vb + 3]),
                               ("sb", sb_done), ("sg", sg_done)] + bw(bn), inc="sp")
                evw = [("sp", cv)] + ([state["vsb_free"],
                                       state["hs_free2"][1]] if q4 == 0 else [])
                ev = em("scalar", lambda s, q4=q4, bn=bn: s.activation(
                    out=Vsb[:, q4, :], in_=bview[bn], func=AF.Identity),
                    waits=evw, inc="sa")
                bank[bn] = ("sa", ev)
            state["vsb_ready"] = ("sa", ev)

        def scores_c01(seg, hp, eq, ek):
            def mk(base, pX):
                def f(t):
                    t.matmul(pX[:, 0, 0:512], K2[base:base + 64, 0:128],
                             Q2[base:base + 64, 0:512], start=True, stop=True,
                             tile_position=(base, 0))
                    return t.matmul(pX[:, 1, 0:384], K2[base:base + 64, 128:256],
                                    Q2[base:base + 64, 128:512], start=True, stop=True,
                                    tile_position=(base, 0))
                return f
            cse_ = em("tensor", mk(0, pA),
                      waits=[("sa", eq), ("sa", ek), ("sv", ps0)] + bw("A0", "A1"),
                      inc="sp")
            cso_ = em("tensor", mk(64, pB), waits=bw("B0", "B1"), inc="sp")
            return cse_, cso_

        def exps_a(seg, hp, cse_, cso_, avdone):
            ce = em("scalar", lambda s: s.activation(
                out=pT[:, 0, 0:1024].rearrange("p (a b) -> p a b", a=2, b=512),
                in_=pA[:, :, :], func=AF.Exp, bias=EXPB, scale=f32(0.125)),
                waits=[("sp", cse_), state["pt_free"][0]], inc="sa")
            bank["A0"] = bank["A1"] = ("sa", ce)
            co = em("scalar", lambda s: s.activation(
                out=pT[:, 1, 0:1024].rearrange("p (a b) -> p a b", a=2, b=512),
                in_=pB[:, :, :], func=AF.Exp, bias=EXPB, scale=f32(0.125)),
                waits=[("sp", cso_), state["pt_free"][1]], inc="sa")
            bank["B0"] = bank["B1"] = ("sa", co)
            return ce, co

        def scores_c23(seg, hp):
            def mk(base, pX):
                def f(t):
                    t.matmul(pX[:, 0, 0:256], K2[base:base + 64, 256:384],
                             Q2[base:base + 64, 256:512], start=True, stop=True,
                             tile_position=(base, 0))
                    return t.matmul(pX[:, 1, 0:128], K2[base:base + 64, 384:512],
                                    Q2[base:base + 64, 384:512], start=True, stop=True,
                                    tile_position=(base, 0))
                return f
            cse_ = em("tensor", mk(0, pA), waits=bw("A0", "A1"), inc="sp")
            cso_ = em("tensor", mk(64, pB), waits=bw("B0", "B1"), inc="sp")
            if hp is not None:
                state["q2_free"] = ("sp", cso_)
                state["k2_free"] = ("sp", cso_)
            return cse_, cso_

        def exps_b(seg, hp, cse_, cso_):
            ce = em("scalar", lambda s: s.activation(
                out=pT[:, 0, 1024:1536].rearrange("p (a b) -> p a b", a=2, b=256),
                in_=pA[:, :, 0:256], func=AF.Exp, bias=EXPB, scale=f32(0.125)),
                waits=[("sp", cse_)], inc="sa")
            bank["A0"] = bank["A1"] = ("sa", ce)
            co = em("scalar", lambda s: s.activation(
                out=pT[:, 1, 1024:1536].rearrange("p (a b) -> p a b", a=2, b=256),
                in_=pB[:, :, 0:256], func=AF.Exp, bias=EXPB, scale=f32(0.125)),
                waits=[("sp", cso_)], inc="sa")
            bank["B0"] = bank["B1"] = ("sa", co)
            return ce, co

        def masks(seg, hp, cea, coa, ceb, cob):
            out = []
            for pi, (ca_, cb_) in ((0, (cea, ceb)), (1, (coa, cob))):
                def mk(pi=pi):
                    def f(v):
                        ins = None
                        for c in range(4):
                            sl = pT[:, pi, PT_OFF[c]:PT_OFF[c] + 128]
                            ins = v.tensor_tensor(sl, sl, utri[:], op=ALU.mult)
                        return ins
                    return f
                out.append(em("vector", mk(), waits=[("sa", ca_), ("sa", cb_),
                                                     ("sg", sg_done)], inc="sv"))
            return out

        def av_head(seg, hp, pi, cmk):
            h = 2 * hp + pi
            obank = "Q" if pi == 0 else "K"
            ovw = oEv if pi == 0 else oOd
            vcol = 64 * h - 512 * (hp // 4)

            def mk_av(pi=pi, vcol=vcol, ovw=ovw):
                def f(t):
                    first = True
                    ins = None
                    for j in range(NQ):
                        for c in range(j + 1):
                            lt = pT[:, pi,
                                    PT_OFF[c] + 128 * (j - c):PT_OFF[c] + 128 * (j - c) + 128]
                            ins = t.matmul(ovw[:, j, 0:64], lt, Vsb[:, c, vcol:vcol + 64],
                                           start=first, stop=False)
                            first = False
                            ins = t.matmul(ovw[:, j, 64:65], lt, onescol[0:P, 0:1],
                                           start=False, stop=(j == NQ - 1 and c == j))
                    return ins
                return f
            cav = em("tensor", mk_av(),
                     waits=[("sv", cmk), state["vsb_ready"]] + bw(obank), inc="sp")
            state["pt_free"][pi] = ("sp", cav)

            crz = em("vector", lambda v, pi=pi, ovw=ovw: v.reciprocal(
                rz[:, pi, :], ovw[:, :, 64]), waits=[("sp", cav)], inc="sv")
            cno = em("vector", lambda v, pi=pi, ovw=ovw: v.tensor_tensor(
                OT[:, 0:NQ, 64 * pi:64 * pi + 64], ovw[:, :, 0:64],
                rz[:, pi, :, None].broadcast_to([P, NQ, 64]), op=ALU.mult),
                waits=[("sv", crz)] + ([state["ot_free"]] if pi == 0 else []),
                inc="sv")
            bank[obank] = ("sv", cno)
            return cno, cav

        def attn_pair_out(seg, hp, cno0, cno1):
            state["trtile"] += 1
            tb = "T%d" % (state["trtile"] % 2)

            def mk_tr(tb=tb):
                def f(t):
                    ins = None
                    for j in range(NQ):
                        ins = t.transpose(bview[tb][:, j, :], OT[:, j, :], ident[:])
                    return ins
                return f
            ctr = em("tensor", mk_tr(), waits=[("sv", cno0), ("sv", cno1)] + bw(tb),
                     inc="sp")
            state["ot_free"] = ("sp", ctr)
            evw = [("sp", ctr)] + ([state["ofm_free"]] if hp == 0 else [])
            cev = em("vector", lambda v, hp=hp, tb=tb: v.tensor_copy(
                OFM[:, hp, :], bview[tb][:, 0:NQ, :]), waits=evw, inc="sv")
            bank[tb] = ("sv", cev)
            return cev

        def attention(seg):
            pend_av = None
            pend_out = None
            last_ofm = None

            def do_av(hp, cmk_e, cmk_o):
                cne, cave = av_head(seg, hp, 0, cmk_e)
                cno, cavo = av_head(seg, hp, 1, cmk_o)
                if hp in (3, 7):
                    state["vsb_free"] = ("sp", cavo)
                    if hp == 7:
                        state["vsb_attn_done"] = ("sp", cavo)
                return cne, cno

            for hp in range(8):
                if pend_out is not None:
                    last_ofm = attn_pair_out(seg, *pend_out)
                    pend_out = None
                if hp == 4:
                    php, cmk_e, cmk_o = pend_av
                    cne, cno = do_av(php, cmk_e, cmk_o)
                    pend_out = (php, cne, cno)
                    pend_av = None
                    v_batch(seg, 1)
                eq, ek = qkv_pair(seg, hp)
                if hp == 0:
                    v_batch(seg, 0)
                cse_, cso_ = scores_c01(seg, hp, eq, ek)
                if pend_av is not None:
                    php, cmk_e, cmk_o = pend_av
                    cne, cno = do_av(php, cmk_e, cmk_o)
                    pend_out = (php, cne, cno)
                    pend_av = None
                cea, coa = exps_a(seg, hp, cse_, cso_, None)
                cse2, cso2 = scores_c23(seg, hp)
                ceb, cob = exps_b(seg, hp, cse2, cso2)
                cmk_e, cmk_o = masks(seg, hp, cea, coa, ceb, cob)
                pend_av = (hp, cmk_e, cmk_o)
                if seg == 0:
                    wpump(3 if hp < 4 else 2)
                ln_pump()
            if pend_out is not None:
                last_ofm = attn_pair_out(seg, *pend_out)
            php, cmk_e, cmk_o = pend_av
            cne, cno = do_av(php, cmk_e, cmk_o)
            last_ofm = attn_pair_out(seg, php, cne, cno)
            state["ofm_ready"][seg] = last_ofm

        pending_tail = [None]

        def ffn_group(seg, r):
            rb = r % 2
            gp_ = (4 * seg + r) % 2
            HSg = HS if gp_ == 0 else Vsb[:, 0:2, :]
            xr = [("sv", state["xn_ready"][(seg, r)])]
            wf = [("swfa", swfa_done)]
            wfe = [("swfb", swfb_done)]

            def mk_rt():
                def f(t):
                    ins = None
                    for q in range(NQ):
                        for ci in range(CI):
                            ins = t.matmul(prt[:, q, :],
                                           XN[:, rb, ci, 128 * q:128 * (q + 1)],
                                           Wr[:, ci, :],
                                           start=(q == 0 and ci == 0),
                                           stop=(q == NQ - 1 and ci == CI - 1))
                    return ins
                return f
            crt = em("tensor", mk_rt(), waits=xr + wf + bw("B0"), inc="sp")
            cz0 = em("scalar", lambda s: s.activation(out=keepb[:], in_=prt[:, :, :],
                                                      func=AF.Tanh, scale=f32(0.5)),
                     waits=[("sp", crt), state["zrt_free"]], inc="sa")
            bank["B0"] = ("sa", cz0)
            cz1 = em("vector", lambda v: v.tensor_scalar(
                zrt[:], keepb[:], -1.0, 1.0, op0=ALU.mult, op1=ALU.add),
                waits=[("sa", cz0)], inc="sv")
            cz2 = em("vector", lambda v: v.reciprocal(wgt[:], zrt[:]),
                     waits=[("sv", cz1)], inc="sv")
            cz = em("vector", lambda v: v.scalar_tensor_tensor(
                out=zrt[:], in0=keepb[:], scalar=1.0, in1=wgt[:],
                op0=ALU.add, op1=ALU.mult), waits=[("sv", cz2)], inc="sv")

            cG = em("vector", lambda v: v.tensor_tensor(
                Grt[:], zrt[:, :, None, :].broadcast_to([P, NQ, E, E]),
                zrt[:, :, :, None].broadcast_to([P, NQ, E, E]), op=ALU.is_gt),
                waits=[("sv", cz)], inc="sv")
            crk = em("vector", lambda v: v.tensor_reduce(rankb[:], Grt[:], axis=AX,
                                                         op=ALU.add),
                     waits=[("sv", cG)], inc="sv")
            cwg = em("vector", lambda v: v.scalar_tensor_tensor(
                wgt[:], rankb[:], 5.5, zrt[:], op0=ALU.is_lt, op1=ALU.mult),
                     waits=[("sv", crk)], inc="sv")
            css = em("vector", lambda v: v.tensor_reduce(ssb[:], wgt[:], axis=AX,
                                                         op=ALU.add),
                     waits=[("sv", cwg)], inc="sv")
            crs = em("vector", lambda v: v.reciprocal(rsb[:], ssb[:]),
                     waits=[("sv", css)], inc="sv")

            def mk_g():
                def f(v):
                    ins = None
                    for q in range(NQ):
                        ins = v.tensor_scalar(gtm[:, q, :], wgt[:, q, :], rsb[:, q:q + 1],
                                              None, op0=ALU.mult)
                    return ins
                return f
            cg = em("vector", mk_g(), waits=[("sv", crs), state["gtm_free"]], inc="sv")
            state["zrt_free"] = ("sv", cg)
            ln_pump()

            ca = {}
            for nm, Wsx, blist in (("a1", Ws1, ("A0", "A1")), ("a2", Ws2, ("B0", "B1"))):
                for k in range(2):
                    bn = blist[k]

                    def mk_a(k=k, bn=bn, Wsx=Wsx):
                        def f(t):
                            ins = None
                            for ci in range(CI):
                                ins = t.matmul(bview[bn], Wsx[:, ci, 128 * k:128 * (k + 1)],
                                               XN[:, rb, ci, :],
                                               start=(ci == 0), stop=(ci == CI - 1))
                            return ins
                        return f
                    ca[(nm, k)] = em("tensor", mk_a(), waits=xr + wf + bw(bn), inc="sp")

            cs_ = {}
            for k in range(2):
                bn = "A%d" % k
                sw = [("sp", ca[("a1", k)])]
                if k == 0:
                    sw.append(state["hs_free2"][gp_])
                    if gp_ == 1:
                        sw.append(state.get("vsb_attn_done"))
                cs_[k] = em("scalar", lambda s, k=k, HSg=HSg: s.activation(
                    out=HSg[:, k, :], in_=pA[:, k, :], func=AF.Silu),
                    waits=sw, inc="sa")
                bank[bn] = ("sa", cs_[k])
            ch = {}
            for k in range(2):
                bn = "B%d" % k
                ch[k] = em("vector", lambda v, k=k, HSg=HSg: v.tensor_tensor(
                    HSg[:, k, :], HSg[:, k, :], pB[:, k, :], op=ALU.mult),
                    waits=[("sa", cs_[k]), ("sp", ca[("a2", k)])], inc="sv")
                bank[bn] = ("sv", ch[k])

            if pending_tail[0] is not None:
                pending_tail[0]()
                pending_tail[0] = None

            pQg = pQ[:, 0:512].rearrange("p (q t) -> p q t", q=NQ, t=P)

            def mk_gt():
                def f(t):
                    ins = None
                    for q in range(NQ):
                        ins = t.transpose(pQg[0:E, q, 0:64].bitcast(BF16), gtm[:, q, :],
                                          ident[:])
                    return ins
                return f
            ctg = em("tensor", mk_gt(), waits=[("sv", cg)] + bw("Q"), inc="sp")
            state["gtm_free"] = ("sp", ctg)
            cgt = em("vector", lambda v: v.tensor_copy(
                gT[:, :, :], pQg[0:E, 0:NQ, 0:64].bitcast(BF16)),
                waits=[("sp", ctg), state["gt_free"]], inc="sv")
            bank["Q"] = ("sv", cgt)
            ln_pump()

            chp_last = None
            last_chp2 = None
            pend_gb = [None]

            def emit_gb(e0, e1, cu):
                cgb0 = em("tensor", lambda t, e=e0: t.matmul(
                    pQ[:, :], sel[0:E, P * e:P * (e + 1)],
                    gT[0:E, :, :].rearrange("a b c -> a (b c)"),
                    start=True, stop=True), waits=[("sv", cgt)] + bw("Q"), inc="sp")
                cgb1 = em("tensor", lambda t, e=e1: t.matmul(
                    pK[:, :], sel[0:E, P * e:P * (e + 1)],
                    gT[0:E, :, :].rearrange("a b c -> a (b c)"),
                    start=True, stop=True), waits=[("sv", cgt)] + bw("K"), inc="sp")
                chp0 = em("vector", lambda v, e=e0: v.tensor_tensor(
                    HP[:, e, :], HP[:, e, :], pQ[:, :], op=ALU.mult),
                    waits=[("sv", cu), ("sp", cgb0)], inc="sv")
                bank["Q"] = ("sv", chp0)
                chp1 = em("vector", lambda v, e=e1: v.tensor_tensor(
                    HP[:, e, :], HP[:, e, :], pK[:, :], op=ALU.mult),
                    waits=[("sp", cgb1)], inc="sv")
                bank["K"] = ("sv", chp1)
                return chp1

            for ep in range(E // 2):
                e0, e1 = 2 * ep, 2 * ep + 1

                def mk_w(Wx, bn, e):
                    def f(t):
                        ins = None
                        for ci in range(CI):
                            ins = t.matmul(bview[bn], Wx[:, e, ci, :],
                                           XN[:, rb, ci, :],
                                           start=(ci == 0), stop=(ci == CI - 1))
                        return ins
                    return f
                w2pair = ("B0", "B1")
                w2view = pB
                cw10 = em("tensor", mk_w(W1, "A0", e0), waits=wfe + bw("A0"), inc="sp")
                cw11 = em("tensor", mk_w(W1, "A1", e1), waits=wfe + bw("A1"), inc="sp")
                cw20 = em("tensor", mk_w(W2, "B0", e0), waits=wfe + bw("B0"), inc="sp")
                cw21 = em("tensor", mk_w(W2, "B1", e1), waits=wfe + bw("B1"), inc="sp")
                if pend_gb[0] is not None:
                    chp_last = emit_gb(*pend_gb[0])
                    pend_gb[0] = None
                cse = em("scalar", lambda s: s.activation(
                    out=s1e[:, :, :], in_=pA[:, :, :], func=AF.Silu),
                    waits=[("sp", cw10), ("sp", cw11), last_chp2], inc="sa")
                bank["A0"] = bank["A1"] = ("sa", cse)
                cu = em("vector", lambda v, e0=e0: v.tensor_tensor(
                    HP[:, e0:e0 + 2, :], s1e[:, :, :], pB[:, :, :], op=ALU.mult),
                    waits=[("sa", cse), ("sp", cw20), ("sp", cw21)] +
                          ([state["hp_free"]] if ep == 0 else []), inc="sv")
                bank["B0"] = bank["B1"] = ("sv", cu)
                pend_gb[0] = (e0, e1, cu)
                last_chp2 = ("sv", cu)
                ln_pump()
            chp_last = emit_gb(*pend_gb[0])
            pend_gb[0] = None

            def tail(seg=seg, r=r, chp_last=chp_last, ch=ch, HSg=HSg, gp_=gp_):
                fin_last = None
                for q in range(NQ):
                    for Hh in range(2):
                        bn = "Q" if Hh == 0 else "K"

                        def mk_fin(q=q, Hh=Hh, bn=bn, r=r):
                            def f(t):
                                ops = []
                                for k in range(2):
                                    ops.append((HSg[:, k, 128 * q:128 * (q + 1)],
                                                Ws3[:, k, 512 * Hh:512 * (Hh + 1)]))
                                for e in range(E):
                                    ops.append((HP[:, e, 128 * q:128 * (q + 1)],
                                                W3[:, e, 512 * Hh:512 * (Hh + 1)]))
                                if r == 0:
                                    for ci in range(CI):
                                        ops.append((OFM[:, ci, 128 * q:128 * (q + 1)],
                                                    Wo[:, ci, 512 * Hh:512 * (Hh + 1)]))
                                    ops.append((ones1[0:1, 0:P],
                                                brow[0:1, 3 * C + 512 * Hh:
                                                     3 * C + 512 * (Hh + 1)]))
                                ins = None
                                for i, (lt, rh) in enumerate(ops):
                                    ins = t.matmul(bview[bn], lt, rh,
                                                   start=(i == 0), stop=(i == len(ops) - 1))
                                return ins
                            return f
                        fw = [("sv", chp_last), ("sv", ch[0]), ("sv", ch[1]),
                              ("swfc", swfc_done)] + bw(bn)
                        if r == 0:
                            fw += [("sv", state["ofm_ready"][seg]), ("sv", wch_ready[31]),
                                   ("sb", sb_done)]
                        cf = em("tensor", mk_fin(), waits=fw, inc="sp")
                        fin_last = cf
                        sl4 = state["fin_slot"] % 4
                        state["fin_slot"] += 1
                        dst = (out_sb[:, sl4, :] if sl4 < 2
                               else WinNat[:, sl4 - 2, :].bitcast(F32))
                        ew = [("sp", cf), ("sst%d" % sl4, 16 * state["st_uses"][sl4])]
                        if sl4 >= 2 and state["st_uses"][sl4] == 0:
                            ew.append(tr_read[sl4 - 2])
                        ce = em("scalar", lambda s, dst=dst, bn=bn: s.activation(
                            out=dst, in_=bview[bn], func=AF.Identity),
                            waits=ew, inc="sa")
                        bank[bn] = ("sa", ce)
                        state["acc_uses"][sl4] += 1
                        em("gpsimd", lambda g, seg=seg, r=r, q=q, Hh=Hh, dst=dst: g.dma_start(
                            out=dst,
                            in_=xv[seg, r, q][:, 512 * Hh:512 * (Hh + 1)],
                            accum_op=ALU.add),
                            waits=[("sa", ce)], inc="sacc%d" % sl4, k=16)
                        em("sync", lambda sy, seg=seg, r=r, q=q, Hh=Hh, dst=dst: sy.dma_start(
                            out=ov[seg, r, q][:, 512 * Hh:512 * (Hh + 1)],
                            in_=dst),
                            waits=[("sacc%d" % sl4, 16 * state["acc_uses"][sl4])],
                            inc="sst%d" % sl4, k=16)
                        state["st_uses"][sl4] += 1
                    ln_pump()
                state["hp_free"] = ("sp", fin_last)
                state["hs_free2"][gp_] = ("sp", fin_last)
                state["gt_free"] = ("sp", fin_last)
                if r == 0:
                    state["ofm_free"] = ("sp", fin_last)
                state["xnbuf_free"][rb] = ("sp", fin_last)
                ffn_done.add((seg, r))

            pending_tail[0] = tail

        def flush_tail():
            if pending_tail[0] is not None:
                pending_tail[0]()
                pending_tail[0] = None

        wensure(16)
        ensure_group(0, 0)
        attention(0)
        wensure(len(witems))
        for seg in range(NSEG):
            for r in range(NR):
                ensure_group(seg, r)
                ffn_group(seg, r)
                if seg == 0 and r == NR - 1:
                    flush_tail()
                    ensure_group(1, 0)
                    attention(1)
        flush_tail()

        em("sync", lambda sy: sy.nop(),
           waits=[("sst%d" % i, 16 * state["st_uses"][i]) for i in range(4)])

        ninstr = {e: len(sch[e]) for e in ENGS}

        with nc.Block() as block:
            @block.sync
            def _(eng):
                for fn in sch["sync"]:
                    fn(eng)

            @block.gpsimd
            def _(eng):
                for fn in sch["gpsimd"]:
                    fn(eng)

            @block.tensor
            def _(eng):
                for fn in sch["tensor"]:
                    fn(eng)

            @block.vector
            def _(eng):
                for fn in sch["vector"]:
                    fn(eng)

            @block.scalar
            def _(eng):
                for fn in sch["scalar"]:
                    fn(eng)

    return nc, sbuf_left, ninstr


def kernel(**inputs):
    xfull = np.ascontiguousarray(inputs["L0_attn_x"])
    nc, _, _ = build_nc()
    base = {
        "w_in": np.ascontiguousarray(inputs["L0_attn_w_in"]),
        "b_in": np.ascontiguousarray(inputs["L0_attn_b_in"]).reshape(1, -1),
        "w_out": np.ascontiguousarray(inputs["L0_attn_w_out"]),
        "b_out": np.ascontiguousarray(inputs["L0_attn_b_out"]).reshape(1, -1),
        "w1": np.ascontiguousarray(inputs["L0_ffn_w1"]),
        "w2": np.ascontiguousarray(inputs["L0_ffn_w2"]),
        "w3": np.ascontiguousarray(inputs["L0_ffn_w3"]),
        "ws1": np.ascontiguousarray(inputs["L0_ffn_ws1"]),
        "ws2": np.ascontiguousarray(inputs["L0_ffn_ws2"]),
        "ws3": np.ascontiguousarray(inputs["L0_ffn_ws3"]),
        "wr": np.ascontiguousarray(inputs["L0_ffn_wr"]),
    }
    in_maps = []
    for k in range(8):
        b, half = k // 2, k % 2
        m = dict(base)
        m["x"] = np.ascontiguousarray(xfull[b, half * 4096:(half + 1) * 4096])
        in_maps.append(m)
    res = run_bass_kernel_spmd(nc, in_maps, core_ids=list(range(8)))
    out = np.empty((4, 8192, 1024), np.float32)
    for k in range(8):
        b, half = k // 2, k % 2
        out[b, half * 4096:(half + 1) * 4096] = res.results[k]["out"]
    return out


if __name__ == "__main__":
    nc, left, ni = build_nc()
    print("sbuf bytes remaining:", left)
    print("instr counts:", ni)
